# version 02
# baseline (speedup 1.0000x reference)
import numpy as np
from contextlib import ExitStack

import concourse.bass as bass
import concourse.mybir as mybir
from concourse.bass_utils import run_bass_kernel_spmd
from concourse.library_overlay import lower_extended_insts

F32 = mybir.dt.float32
BF16 = mybir.dt.bfloat16
U32 = mybir.dt.uint32
AF = mybir.ActivationFunctionType
ALU = mybir.AluOpType
AX = mybir.AxisListType

D = 1024
DC = D // 128
NE = 18
I = 512
TBLK = 2048
NT = TBLK // 128
QB = 1024
NQB = 4
KC = 1024
NKC = 16384 // KC
EPS = 1e-5


class Sem:
    def __init__(self, handle, step):
        self.h = handle
        self.step = step
        self.n = 0

    def bump(self):
        self.n += self.step
        return self.n


class Tags:

    def __init__(self):
        self.d = {}
        self.final = False

    def set(self, key, val):
        if self.final:
            assert self.d.get(key) == val, f"tag drift {key}: {self.d.get(key)} != {val}"
        else:
            self.d[key] = val
        return val

    def get(self, key):
        return self.d.get(key, 0)


def build_program(T: Tags, mode: str = "full"):
    nc = bass.Bass("TRN2", target_bir_lowering=False, debug=False)

    x_in = nc.dram_tensor("x_in", [8192, D], F32, kind="ExternalInput")
    gw_in = nc.dram_tensor("gw", [D, 16], F32, kind="ExternalInput")
    gb_in = nc.dram_tensor("gb", [1, 16], F32, kind="ExternalInput")
    wg_in = nc.dram_tensor("wg_all", [NE, D, I], F32, kind="ExternalInput")
    wu_in = nc.dram_tensor("wu_all", [NE, D, I], F32, kind="ExternalInput")
    wd_in = nc.dram_tensor("wd_all", [NE, I, D], F32, kind="ExternalInput")
    qk_in = nc.dram_tensor("qk", [D, D], F32, kind="ExternalInput")
    vo_in = nc.dram_tensor("vo", [D, D], F32, kind="ExternalInput")
    id_in = nc.dram_tensor("ident", [128, 128], F32, kind="ExternalInput")
    role_in = nc.dram_tensor("role", [1, 16], U32, kind="ExternalInput")
    out_d = nc.dram_tensor("out", [4096, D], F32, kind="ExternalOutput")
    dbg = {}
    if mode == "ffn1dbg":
        dbg["zt"] = nc.dram_tensor("dbg_zt", [128, DC * TBLK], BF16, kind="ExternalOutput")
        dbg["gmat"] = nc.dram_tensor("dbg_gmat", [128, 256], F32, kind="ExternalOutput")
        dbg["cw"] = nc.dram_tensor("dbg_cw", [128, 256], F32, kind="ExternalOutput")
        dbg["yb"] = nc.dram_tensor("dbg_yb", [128, NT * 1024], F32, kind="ExternalOutput")
        dbg["st1"] = nc.dram_tensor("dbg_st1", [128, NT * 4], F32, kind="ExternalOutput")
        dbg["act"] = nc.dram_tensor("dbg_act", [128, 4 * TBLK], BF16, kind="ExternalOutput")

    x1_loc = nc.dram_tensor("x1_loc", [4096, D], F32)
    z2tm_loc = nc.dram_tensor("z2tm_loc", [8192, D], BF16)
    z2fm_loc = nc.dram_tensor("z2fm_loc", [D, 8192], BF16)
    x2_loc = nc.dram_tensor("x2_loc", [4096, D], F32)
    z3_fm = nc.dram_tensor("z3_fm", [D, 4096], BF16)

    g_fm = nc.dram_tensor("g_fm", [D, 16384], BF16, addr_space="Shared")
    g_tm = nc.dram_tensor("g_tm", [16384, D], BF16, addr_space="Shared")
    flags = nc.dram_tensor("flags", [1, 2], U32, addr_space="Shared")

    ctx = ExitStack()
    with ctx:
        En = ctx.enter_context
        sW = Sem(En(nc.semaphore("sW")), 16)
        sX = Sem(En(nc.semaphore("sX")), 16)
        sG = Sem(En(nc.semaphore("sG")), 16)
        sPE = Sem(En(nc.semaphore("sPE")), 1)
        sVE = Sem(En(nc.semaphore("sVE")), 1)
        sAC = Sem(En(nc.semaphore("sAC")), 1)
        sRDY = Sem(En(nc.semaphore("sRDY")), 1)
        sPOLL = En(nc.semaphore("sPOLL"))

        gwT = En(nc.sbuf_tensor("gwT", [128, DC, 16], BF16))
        bias256 = En(nc.sbuf_tensor("bias256", [128, 256], F32))
        identb = En(nc.sbuf_tensor("identb", [128, 128], BF16))
        ones1 = En(nc.sbuf_tensor("ones1", [1, 128], F32))
        onesb = En(nc.sbuf_tensor("onesb", [128, 1], BF16))
        epsb = En(nc.sbuf_tensor("epsb", [128, 1], F32))
        stats3 = En(nc.sbuf_tensor("stats3", [128, 32, 2], F32))
        role_s = En(nc.sbuf_tensor("role_s", [1, 16], U32))
        flg1 = En(nc.sbuf_tensor("flg1", [1, 1], U32))

        block = En(nc.Block())

        def snap():
            return {"pe": sPE.n, "ve": sVE.n, "ac": sAC.n,
                    "w": sW.n, "x": sX.n, "g": sG.n}

        def barrier(eng, prev, skip=()):
            for nm, sem in (("pe", sPE), ("ve", sVE), ("ac", sAC),
                            ("w", sW), ("x", sX), ("g", sG)):
                if nm in skip or prev[nm] == 0:
                    continue
                eng.wait_ge(sem.h, prev[nm])

        @block.gpsimd
        def _(g):
            g.dma_start(out=role_s[:, :], in_=role_in.ap()).then_inc(sG.h, 16)
            sG.bump()
            g.dma_start(out=identb[:, :], in_=id_in.ap()).then_inc(sG.h, 16)
            sG.bump()
            g.dma_start(out=gwT[:, :, :],
                        in_=gw_in.ap().rearrange("(p c) e -> p c e", p=128)).then_inc(sG.h, 16)
            sG.bump()
            g.dma_start(out=bias256[0:1, 0:16], in_=gb_in.ap()).then_inc(sG.h, 16)
            sG.bump()
            g.wait_ge(sG.h, sG.n)
        p0_g = sG.n

        @block.vector
        def _(v):
            v.memset(ones1[:, :], 1.0)
            v.memset(epsb[:, :], EPS)
            v.memset(onesb[:, :], 1.0).then_inc(sVE.h, 1)
            sVE.bump()
        ones_tag = sVE.n

        with nc.psum_tensor("pb0", [128, 16], F32) as pb0:
            @block.tensor
            def _(t):
                t.wait_ge(sG.h, p0_g)
                t.wait_ge(sVE.h, ones_tag)
                t.matmul(pb0[:, :], ones1[0:1, :], bias256[0:1, 0:16],
                         start=True, stop=True).then_inc(sPE.h, 1)
                sPE.bump()
            btag = sPE.n

            @block.scalar
            def _(s):
                s.wait_ge(sPE.h, btag)
                for rep in range(16):
                    ins = s.copy(bias256[:, 16 * rep:16 * (rep + 1)], pb0[:, :])
                ins.then_inc(sAC.h, 1)
                sAC.bump()
        bias_tag = sAC.n

        prev = snap()

        def emit_ffn_block(tag, epi):
            nonlocal prev
            ffn1 = epi["kind"] == "ffn1"
            ps = ExitStack()
            with ps:
                Ep = ps.enter_context
                zT = Ep(nc.sbuf_tensor(f"zT{tag}", [128, DC, TBLK], BF16))
                yb = Ep(nc.sbuf_tensor(f"y{tag}", [128, NT, 1024], F32))
                actb = Ep(nc.sbuf_tensor(f"act{tag}", [128, 4, TBLK], BF16))
                wgs = Ep(nc.sbuf_tensor(f"wgs{tag}", [128, 2, DC, I], BF16))
                wus = Ep(nc.sbuf_tensor(f"wus{tag}", [128, 2, DC, I], BF16))
                wds = Ep(nc.sbuf_tensor(f"wds{tag}", [128, 2, 4, 1024], BF16))
                xb = Ep(nc.sbuf_tensor(f"xb{tag}", [128, 2, 1024], F32))
                sqb = Ep(nc.sbuf_tensor(f"sq{tag}", [128, 1024], F32))
                ztm = Ep(nc.sbuf_tensor(f"ztm{tag}", [128, 2, 1024], BF16))
                zhf = Ep(nc.sbuf_tensor(f"zhf{tag}", [128, 2, 1024], F32))
                z2T = Ep(nc.sbuf_tensor(f"z2T{tag}", [128, 2, DC, 128], BF16))
                st1 = Ep(nc.sbuf_tensor(f"st1{tag}", [128, NT, 4], F32))
                gmat = Ep(nc.sbuf_tensor(f"G{tag}", [128, 256], F32))
                sfc = Ep(nc.sbuf_tensor(f"sfc{tag}", [128, 256], F32))
                mskd = Ep(nc.sbuf_tensor(f"mskd{tag}", [128, 256], F32))
                cw = Ep(nc.sbuf_tensor(f"cw{tag}", [128, 256], F32))
                gsc = Ep(nc.sbuf_tensor(f"gsc{tag}", [128, 10, 16], F32))
                x1e = Ep(nc.sbuf_tensor(f"x1e{tag}", [128, 2, 1024], F32))
                pz = Ep(nc.psum_tensor(f"pz{tag}", [128, 1024], BF16))
                plg = Ep(nc.psum_tensor(f"plg{tag}", [128, 256], F32))
                pup = [Ep(nc.psum_tensor(f"pup{tag}{i_}", [128, I], F32)) for i_ in range(4)]
                pdn = [Ep(nc.psum_tensor(f"pdn{tag}{i_}", [128, I], F32)) for i_ in range(2)]

                xrow0 = epi["xrow0"]

                if ffn1:
                    @block.scalar
                    def _(s):
                        barrier(s, prev, skip=("ac", "x"))
                        for t_ in range(NT):
                            if t_ >= 2:
                                s.wait_ge(sVE.h, T.get((tag, "z", t_ - 2)))
                            s.dma_start(out=xb[:, t_ % 2, :],
                                        in_=x_in.ap()[xrow0 + 128 * t_: xrow0 + 128 * (t_ + 1), :]
                                        ).then_inc(sX.h, 16)
                            T.set((tag, "x", t_), sX.bump())
                            s.wait_ge(sX.h, T.get((tag, "x", t_)))
                            if t_ >= 1:
                                s.wait_ge(sVE.h, T.get((tag, "var", t_ - 1)))
                            s.activation(sqb[:, :], xb[:, t_ % 2, :], AF.Square).then_inc(sAC.h, 1)
                            T.set((tag, "sq", t_), sAC.bump())
                            s.wait_ge(sVE.h, T.get((tag, "var", t_)))
                            s.activation(st1[:, t_, 2:3], st1[:, t_, 1:2], AF.Sqrt,
                                         bias=epsb[:, 0:1]).then_inc(sAC.h, 1)
                            T.set((tag, "sqrt", t_), sAC.bump())
                            s.wait_ge(sPE.h, T.get((tag, "ztr", t_)))
                            s.copy(zT[:, :, 128 * t_:128 * (t_ + 1)],
                                   pz.ap().rearrange("p (c q) -> p c q", q=128)).then_inc(sAC.h, 1)
                            T.set((tag, "zev", t_), sAC.bump())

                    @block.vector
                    def _(v):
                        def CH(ins):
                            ins.then_inc(sVE.h, 1)
                            c = sVE.bump()
                            v.wait_ge(sVE.h, c)
                            return c
                        barrier(v, prev, skip=("ve",))
                        for t_ in range(NT):
                            v.wait_ge(sX.h, T.get((tag, "x", t_)))
                            xt = xb[:, t_ % 2, :]
                            CH(v.tensor_reduce(st1[:, t_, 0:1], xt, AX.X, ALU.add))
                            CH(v.tensor_scalar(st1[:, t_, 0:1], st1[:, t_, 0:1], -1.0 / D, None, ALU.mult))
                            v.wait_ge(sAC.h, T.get((tag, "sq", t_)))
                            CH(v.tensor_reduce(st1[:, t_, 1:2], sqb[:, :], AX.X, ALU.add))
                            CH(v.tensor_scalar(st1[:, t_, 1:2], st1[:, t_, 1:2], 1.0 / D, None, ALU.mult))
                            CH(v.tensor_tensor(st1[:, t_, 3:4], st1[:, t_, 0:1], st1[:, t_, 0:1], ALU.mult))
                            T.set((tag, "var", t_),
                                  CH(v.tensor_sub(st1[:, t_, 1:2], st1[:, t_, 1:2], st1[:, t_, 3:4])))
                            v.wait_ge(sAC.h, T.get((tag, "sqrt", t_)))
                            CH(v.reciprocal(st1[:, t_, 1:2], st1[:, t_, 2:3]))
                            if t_ >= 2:
                                v.wait_ge(sPE.h, T.get((tag, "ztr", t_ - 2)))
                            v.tensor_scalar(ztm[:, t_ % 2, :], xt, st1[:, t_, 0:1], st1[:, t_, 1:2],
                                            ALU.add, ALU.mult).then_inc(sVE.h, 1)
                            T.set((tag, "z", t_), sVE.bump())

                    @block.tensor
                    def _(t):
                        barrier(t, prev, skip=("pe",))
                        for t_ in range(NT):
                            t.wait_ge(sVE.h, T.get((tag, "z", t_)))
                            if t_ >= 1:
                                t.wait_ge(sAC.h, T.get((tag, "zev", t_ - 1)))
                            for dcc in range(DC):
                                ins = t.transpose(pz[:, 128 * dcc:128 * (dcc + 1)],
                                                  ztm[:, t_ % 2, 128 * dcc:128 * (dcc + 1)],
                                                  identb[:, :])
                            ins.then_inc(sPE.h, 1)
                            T.set((tag, "ztr", t_), sPE.bump())

                    def wait_zt(eng, t_):
                        eng.wait_ge(sAC.h, T.get((tag, "zev", t_)))

                    def wait_zt_all(eng):
                        eng.wait_ge(sAC.h, T.get((tag, "zev", NT - 1)))
                else:
                    zcol0 = epi["blk"] * TBLK

                    @block.sync
                    def _(sy):
                        barrier(sy, prev, skip=("w",))
                        sy.dma_start(
                            out=zT[:, :, :],
                            in_=z3_fm.ap().rearrange("(c p) n -> p c n", p=128)[:, :, zcol0:zcol0 + TBLK]
                        ).then_inc(sW.h, 16)
                        T.set((tag, "ztload"), sW.bump())

                    def wait_zt(eng, t_):
                        eng.wait_ge(sW.h, T.get((tag, "ztload")))

                    def wait_zt_all(eng):
                        eng.wait_ge(sW.h, T.get((tag, "ztload")))

                @block.tensor
                def _(t):
                    if not ffn1:
                        barrier(t, prev, skip=("pe",))
                    t.wait_ge(sAC.h, bias_tag)
                    for t_ in range(NT):
                        wait_zt(t, t_)
                        for dcc in range(DC):
                            ins = t.matmul(plg[:, 16 * t_:16 * (t_ + 1)],
                                           zT[:, dcc, 128 * t_:128 * (t_ + 1)], gwT[:, dcc, :],
                                           start=(dcc == 0), stop=(dcc == DC - 1))
                    ins.then_inc(sPE.h, 1)
                    T.set((tag, "lg"), sPE.bump())

                @block.scalar
                def _(s):
                    if not ffn1:
                        barrier(s, prev, skip=("ac", "x"))
                    s.wait_ge(sPE.h, T.get((tag, "lg")))
                    s.activation(gmat[:, :], plg[:, :], AF.Sigmoid).then_inc(sAC.h, 1)
                    T.set((tag, "sig"), sAC.bump())

                @block.vector
                def _(v):
                    def CH(ins):
                        ins.then_inc(sVE.h, 1)
                        c = sVE.bump()
                        v.wait_ge(sVE.h, c)
                        return c
                    if not ffn1:
                        barrier(v, prev, skip=("ve",))
                    v.wait_ge(sAC.h, T.get((tag, "sig")))
                    CH(v.tensor_add(sfc[:, :], gmat[:, :], bias256[:, :]))
                    for gi in range(4):
                        c = [sfc[:, 4 * gi + j::16] for j in range(4)]
                        for k_, (a, b) in enumerate([(0, 1), (0, 2), (0, 3), (1, 2), (1, 3), (2, 3)]):
                            CH(v.tensor_add(gsc[:, 4, :], c[a], c[b]))
                            if k_ == 0:
                                CH(v.tensor_copy(gsc[:, gi, :], gsc[:, 4, :]))
                            else:
                                CH(v.tensor_max(gsc[:, gi, :], gsc[:, gi, :], gsc[:, 4, :]))
                    for k_, (a, b) in enumerate([(0, 1), (0, 2), (0, 3), (1, 2), (1, 3), (2, 3)]):
                        CH(v.tensor_tensor(gsc[:, 5, :], gsc[:, a, :], gsc[:, b, :], ALU.min))
                        if k_ == 0:
                            CH(v.tensor_copy(gsc[:, 6, :], gsc[:, 5, :]))
                        else:
                            CH(v.tensor_max(gsc[:, 6, :], gsc[:, 6, :], gsc[:, 5, :]))
                    for gi in range(4):
                        CH(v.tensor_tensor(gsc[:, gi, :], gsc[:, gi, :], gsc[:, 6, :], ALU.is_ge))
                        CH(v.tensor_scalar(gsc[:, gi, :], gsc[:, gi, :], -1.0, 1e4, ALU.add, ALU.mult))
                    for e in range(16):
                        CH(v.tensor_add(mskd[:, e::16], sfc[:, e::16], gsc[:, e // 4, :]))
                    for rnd in range(4):
                        CH(v.tensor_tensor(gsc[:, 7, :], mskd[:, 0::16], mskd[:, 1::16], ALU.max))
                        for e in range(2, 16):
                            CH(v.tensor_max(gsc[:, 7, :], gsc[:, 7, :], mskd[:, e::16]))
                        for e in range(16):
                            if rnd == 0:
                                CH(v.tensor_tensor(cw[:, e::16], mskd[:, e::16], gsc[:, 7, :], ALU.is_ge))
                                CH(v.scalar_tensor_tensor(mskd[:, e::16], cw[:, e::16], -2e4,
                                                       mskd[:, e::16], ALU.mult, ALU.add))
                            else:
                                CH(v.tensor_tensor(gsc[:, 8, :], mskd[:, e::16], gsc[:, 7, :], ALU.is_ge))
                                CH(v.tensor_add(cw[:, e::16], cw[:, e::16], gsc[:, 8, :]))
                                if rnd < 3:
                                    CH(v.scalar_tensor_tensor(mskd[:, e::16], gsc[:, 8, :], -2e4,
                                                           mskd[:, e::16], ALU.mult, ALU.add))
                    CH(v.tensor_mul(cw[:, :], cw[:, :], gmat[:, :]))
                    CH(v.tensor_add(gsc[:, 9, :], cw[:, 0::16], cw[:, 1::16]))
                    for e in range(2, 16):
                        CH(v.tensor_add(gsc[:, 9, :], gsc[:, 9, :], cw[:, e::16]))
                    CH(v.tensor_scalar(gsc[:, 9, :], gsc[:, 9, :], 1e-20, None, ALU.add))
                    CH(v.reciprocal(gsc[:, 9, :], gsc[:, 9, :]))
                    CH(v.tensor_scalar(gsc[:, 9, :], gsc[:, 9, :], 2.5, None, ALU.mult))
                    for e in range(16):
                        last = CH(v.tensor_mul(cw[:, e::16], cw[:, e::16], gsc[:, 9, :]))
                    T.set((tag, "cw"), last)

                def up2(e, step):
                    k = 16 * e + step
                    return (k - 2) // 16, (k - 2) % 16

                def dn2(e, st2):
                    k = 32 * e + st2
                    return (k - 2) // 32, (k - 2) % 32

                @block.gpsimd
                def _(g):
                    barrier(g, prev, skip=("g",))
                    for e in range(NE):
                        par = e % 2
                        if e >= 2:
                            g.wait_ge(sPE.h, T.get((tag, "eslot", e - 2)))
                        g.dma_start(out=wgs[:, par, :, :],
                                    in_=wg_in.ap()[e].rearrange("(p c) i -> p c i", p=128)
                                    ).then_inc(sG.h, 16)
                        sG.bump()
                        g.dma_start(out=wus[:, par, :, :],
                                    in_=wu_in.ap()[e].rearrange("(p c) i -> p c i", p=128)
                                    ).then_inc(sG.h, 16)
                        sG.bump()
                        g.dma_start(out=wds[:, par, :, :],
                                    in_=wd_in.ap()[e].rearrange("(p c) h -> p c h", p=128)
                                    ).then_inc(sG.h, 16)
                        T.set((tag, "w", e), sG.bump())

                @block.tensor
                def _(t):
                    wait_zt_all(t)
                    for e in range(NE):
                        par = e % 2
                        t.wait_ge(sG.h, T.get((tag, "w", e)))
                        for step in range(16):
                            sl, ic = step // 4, step % 4
                            pp = step % 2
                            if 16 * e + step >= 2:
                                ep, sp = up2(e, step)
                                t.wait_ge(sAC.h, T.get((tag, "silu", ep, sp)))
                                t.wait_ge(sVE.h, T.get((tag, "mult", ep, sp)))
                            for dcc in range(DC):
                                ins = t.matmul(pup[2 * pp][:, :],
                                               wgs[:, par, dcc, 128 * ic:128 * (ic + 1)],
                                               zT[:, dcc, 512 * sl:512 * (sl + 1)],
                                               start=(dcc == 0), stop=(dcc == DC - 1))
                            ins.then_inc(sPE.h, 1)
                            T.set((tag, "hg", e, step), sPE.bump())
                            for dcc in range(DC):
                                ins = t.matmul(pup[2 * pp + 1][:, :],
                                               wus[:, par, dcc, 128 * ic:128 * (ic + 1)],
                                               zT[:, dcc, 512 * sl:512 * (sl + 1)],
                                               start=(dcc == 0), stop=(dcc == DC - 1))
                            ins.then_inc(sPE.h, 1)
                            T.set((tag, "hu", e, step), sPE.bump())
                        for t_ in range(NT):
                            sl = t_ // 4
                            for h2 in range(2):
                                st2 = 2 * t_ + h2
                                pp = st2 % 2
                                if h2 == 0 and t_ % 4 == 0:
                                    t.wait_ge(sVE.h, T.get((tag, "mult", e, 4 * sl + 3)))
                                if 32 * e + st2 >= 2:
                                    ep, sp = dn2(e, st2)
                                    t.wait_ge(sVE.h, T.get((tag, "yev", ep, sp)))
                                for ic in range(4):
                                    ins = t.matmul(pdn[pp][:, :],
                                                   actb[:, ic, 128 * t_:128 * (t_ + 1)],
                                                   wds[:, par, ic, 512 * h2:512 * (h2 + 1)],
                                                   start=(ic == 0), stop=(ic == 3))
                                ins.then_inc(sPE.h, 1)
                                T.set((tag, "dn", e, st2), sPE.bump())
                        T.set((tag, "eslot", e), T.get((tag, "dn", e, 31)))

                @block.scalar
                def _(s):
                    for e in range(NE):
                        for step in range(16):
                            s.wait_ge(sPE.h, T.get((tag, "hg", e, step)))
                            if 16 * e + step >= 2:
                                ep, sp = up2(e, step)
                                s.wait_ge(sVE.h, T.get((tag, "mult", ep, sp)))
                            s.activation(sqb[:, 512 * (step % 2):512 * (step % 2 + 1)],
                                         pup[2 * (step % 2)][:, :],
                                         AF.Silu).then_inc(sAC.h, 1)
                            T.set((tag, "silu", e, step), sAC.bump())

                @block.vector
                def _(v):
                    for e in range(NE):
                        if e >= 1:
                            v.wait_ge(sPE.h, T.get((tag, "eslot", e - 1)))
                        for step in range(16):
                            sl, ic = step // 4, step % 4
                            v.wait_ge(sPE.h, T.get((tag, "hu", e, step)))
                            v.wait_ge(sAC.h, T.get((tag, "silu", e, step)))
                            v.tensor_tensor(actb[:, ic, 512 * sl:512 * (sl + 1)],
                                            sqb[:, 512 * (step % 2):512 * (step % 2 + 1)],
                                            pup[2 * (step % 2) + 1][:, :],
                                            ALU.mult).then_inc(sVE.h, 1)
                            T.set((tag, "mult", e, step), sVE.bump())
                        if e == 0:
                            v.wait_ge(sVE.h, T.get((tag, "cw")))
                        for t_ in range(NT):
                            for h2 in range(2):
                                st2 = 2 * t_ + h2
                                v.wait_ge(sPE.h, T.get((tag, "dn", e, st2)))
                                ysl = yb[:, t_, 512 * h2:512 * (h2 + 1)]
                                if e == 0:
                                    ins = v.tensor_scalar(ysl, pdn[st2 % 2][:, :],
                                                          cw[:, 16 * t_:16 * t_ + 1], None, ALU.mult)
                                elif e < 16:
                                    ins = v.scalar_tensor_tensor(ysl, pdn[st2 % 2][:, :],
                                                                 cw[:, 16 * t_ + e:16 * t_ + e + 1],
                                                                 ysl, ALU.mult, ALU.add)
                                else:
                                    ins = v.tensor_add(ysl, ysl, pdn[st2 % 2][:, :])
                                ins.then_inc(sVE.h, 1)
                                T.set((tag, "yev", e, st2), sVE.bump())

                if ffn1:
                    erow_src, erow0 = x_in, xrow0
                else:
                    erow_src, erow0 = x2_loc, epi["blk"] * TBLK

                @block.scalar
                def _(s):
                    for t_ in range(NT):
                        if t_ >= 2:
                            s.wait_ge(sVE.h, T.get((tag, "ex1", t_ - 2)))
                        elif ffn1:
                            s.wait_ge(sVE.h, T.get((tag, "z", NT - 2 + t_)))
                        s.dma_start(out=xb[:, t_ % 2, :],
                                    in_=erow_src.ap()[erow0 + 128 * t_: erow0 + 128 * (t_ + 1), :]
                                    ).then_inc(sX.h, 16)
                        T.set((tag, "ex", t_), sX.bump())
                        if ffn1:
                            s.wait_ge(sVE.h, T.get((tag, "ex1", t_)))
                            if t_ >= 1:
                                s.wait_ge(sVE.h, T.get((tag, "evar", t_ - 1)))
                            s.activation(sqb[:, :], x1e[:, t_ % 2, :], AF.Square).then_inc(sAC.h, 1)
                            T.set((tag, "esq", t_), sAC.bump())
                            s.wait_ge(sVE.h, T.get((tag, "evar", t_)))
                            s.activation(st1[:, t_, 2:3], st1[:, t_, 1:2], AF.Sqrt,
                                         bias=epsb[:, 0:1]).then_inc(sAC.h, 1)
                            T.set((tag, "esqrt", t_), sAC.bump())
                            s.wait_ge(sPE.h, T.get((tag, "eztr", t_)))
                            if t_ >= 2:
                                s.wait_ge(sW.h, T.get((tag, "gfm", t_ - 2)))
                            s.copy(z2T[:, t_ % 2, :, :],
                                   pz.ap().rearrange("p (c q) -> p c q", q=128)).then_inc(sAC.h, 1)
                            T.set((tag, "ez2ev", t_), sAC.bump())

                @block.vector
                def _(v):
                    def CH(ins):
                        ins.then_inc(sVE.h, 1)
                        c = sVE.bump()
                        v.wait_ge(sVE.h, c)
                        return c
                    for t_ in range(NT):
                        v.wait_ge(sX.h, T.get((tag, "ex", t_)))
                        xt = xb[:, t_ % 2, :]
                        if ffn1:
                            negm, rstd = st1[:, t_, 0:1], st1[:, t_, 1:2]
                        else:
                            sidx = epi["blk"] * NT + t_
                            negm, rstd = stats3[:, sidx, 0:1], stats3[:, sidx, 1:2]
                        hs = st1[:, t_, 3:4]
                        CH(v.tensor_scalar(hs, rstd, 0.5, None, ALU.mult))
                        if t_ >= 2:
                            v.wait_ge(sVE.h, T.get((tag, "ex1", t_ - 2)))
                        CH(v.tensor_scalar(zhf[:, t_ % 2, :], xt, negm, hs, ALU.add, ALU.mult))
                        if t_ >= 2:
                            v.wait_ge(sW.h, T.get((tag, "x1st", t_ - 2)))
                            if ffn1:
                                v.wait_ge(sAC.h, T.get((tag, "esq", t_ - 2)))
                        CH(v.scalar_tensor_tensor(x1e[:, t_ % 2, :], yb[:, t_, :], 0.5, xt,
                                                  ALU.mult, ALU.add))
                        ins = v.tensor_add(x1e[:, t_ % 2, :], x1e[:, t_ % 2, :], zhf[:, t_ % 2, :])
                        T.set((tag, "ex1", t_), CH(ins))
                        if ffn1:
                            CH(v.tensor_reduce(st1[:, t_, 0:1], x1e[:, t_ % 2, :], AX.X, ALU.add))
                            CH(v.tensor_scalar(st1[:, t_, 0:1], st1[:, t_, 0:1], -1.0 / D, None, ALU.mult))
                            v.wait_ge(sAC.h, T.get((tag, "esq", t_)))
                            CH(v.tensor_reduce(st1[:, t_, 1:2], sqb[:, :], AX.X, ALU.add))
                            CH(v.tensor_scalar(st1[:, t_, 1:2], st1[:, t_, 1:2], 1.0 / D, None, ALU.mult))
                            CH(v.tensor_tensor(st1[:, t_, 3:4], st1[:, t_, 0:1], st1[:, t_, 0:1], ALU.mult))
                            T.set((tag, "evar", t_),
                                  CH(v.tensor_sub(st1[:, t_, 1:2], st1[:, t_, 1:2], st1[:, t_, 3:4])))
                            v.wait_ge(sAC.h, T.get((tag, "esqrt", t_)))
                            CH(v.reciprocal(st1[:, t_, 1:2], st1[:, t_, 2:3]))
                            if t_ >= 2:
                                v.wait_ge(sW.h, T.get((tag, "gtm", t_ - 2)))
                                v.wait_ge(sPE.h, T.get((tag, "eztr", t_ - 2)))
                            ins = v.tensor_scalar(ztm[:, t_ % 2, :], x1e[:, t_ % 2, :],
                                                  st1[:, t_, 0:1], st1[:, t_, 1:2],
                                                  ALU.add, ALU.mult)
                            ins.then_inc(sVE.h, 1)
                            T.set((tag, "ez2", t_), sVE.bump())

                if ffn1:
                    @block.tensor
                    def _(t):
                        t.wait_ge(sAC.h, T.get((tag, "zev", NT - 1)))
                        for t_ in range(NT):
                            t.wait_ge(sVE.h, T.get((tag, "ez2", t_)))
                            if t_ >= 1:
                                t.wait_ge(sAC.h, T.get((tag, "ez2ev", t_ - 1)))
                            for dcc in range(DC):
                                ins = t.transpose(pz[:, 128 * dcc:128 * (dcc + 1)],
                                                  ztm[:, t_ % 2, 128 * dcc:128 * (dcc + 1)],
                                                  identb[:, :])
                            ins.then_inc(sPE.h, 1)
                            T.set((tag, "eztr", t_), sPE.bump())

                    @block.sync
                    def _(sy):
                        barrier(sy, prev, skip=("w",))
                        col0 = epi["bpos"] * TBLK
                        for t_ in range(NT):
                            sy.wait_ge(sAC.h, T.get((tag, "ez2ev", t_)))
                            sy.dma_start(
                                out=z2fm_loc.ap().rearrange("(c p) n -> p c n", p=128)[:, :, col0 + 128 * t_: col0 + 128 * (t_ + 1)],
                                in_=z2T[:, t_ % 2, :, :]).then_inc(sW.h, 16)
                            T.set((tag, "gfm", t_), sW.bump())
                            sy.wait_ge(sVE.h, T.get((tag, "ez2", t_)))
                            sy.dma_start(
                                out=z2tm_loc.ap()[col0 + 128 * t_: col0 + 128 * (t_ + 1), :],
                                in_=ztm[:, t_ % 2, :]).then_inc(sW.h, 16)
                            T.set((tag, "gtm", t_), sW.bump())
                            if epi["qblock"]:
                                sy.wait_ge(sVE.h, T.get((tag, "ex1", t_)))
                                sy.dma_start(out=x1_loc.ap()[epi["blk"] * TBLK + 128 * t_:
                                                             epi["blk"] * TBLK + 128 * (t_ + 1), :],
                                             in_=x1e[:, t_ % 2, :]).then_inc(sW.h, 16)
                                T.set((tag, "x1st", t_), sW.bump())
                            else:
                                T.set((tag, "x1st", t_), 0)
                else:
                    @block.sync
                    def _(sy):
                        barrier(sy, prev, skip=("w",))
                        for t_ in range(NT):
                            sy.wait_ge(sVE.h, T.get((tag, "ex1", t_)))
                            sy.dma_start(out=out_d.ap()[erow0 + 128 * t_: erow0 + 128 * (t_ + 1), :],
                                         in_=x1e[:, t_ % 2, :]).then_inc(sW.h, 16)
                            T.set((tag, "x1st", t_), sW.bump())

                if dbg and tag == "f1b0":
                    @block.sync
                    def _(sy):
                        sy.wait_ge(sVE.h, sVE.n)
                        sy.wait_ge(sAC.h, sAC.n)
                        sy.wait_ge(sPE.h, sPE.n)
                        sy.wait_ge(sX.h, sX.n)
                        for name, tile in (("zt", zT), ("gmat", gmat), ("cw", cw),
                                           ("yb", yb), ("st1", st1), ("act", actb)):
                            sy.dma_start(out=dbg[name].ap(),
                                         in_=tile[:].rearrange(
                                             "p a b -> p (a b)" if len(tile.shape) == 3 else "p a -> p (a)")
                                         ).then_inc(sW.h, 16)
                            sW.bump()

                prev = snap()

        def emit_pair_barrier():
            nonlocal prev

            @block.sync
            def _(sy):
                sy.wait_ge(sW.h, sW.n)
                p = sy.value_load(role_s[0:1, 1:2], min_val=0, max_val=1)
                sy.dma_start(
                    out=g_tm.ap().rearrange("(s r) d -> s r d", r=8192)[bass.ds(p, 1)],
                    in_=z2tm_loc.ap().rearrange("(a r) d -> a r d", a=1)).then_inc(sW.h, 16)
                sW.bump()
                sy.dma_start(
                    out=g_fm.ap().rearrange("d (s n) -> d s n", n=8192)[:, bass.ds(p, 1), :],
                    in_=z2fm_loc.ap().rearrange("d (a n) -> d a n", a=1)).then_inc(sW.h, 16)
                sW.bump()

            @block.gpsimd
            def _(g):
                g.wait_ge(sW.h, sW.n)
                p = g.value_load(role_s[0:1, 1:2], min_val=0, max_val=1)
                g.dma_start(out=flags.ap()[:, bass.ds(p, 1)],
                            in_=role_s[0:1, 0:1]).then_inc(sG.h, 16)
                g.wait_ge(sG.h, sG.bump())
                pa = g.value_load(role_s[0:1, 2:3], min_val=0, max_val=1)
                nonce_v = g.value_load(role_s[0:1, 0:1], min_val=0)
                with g.register("pollc") as pc:
                    g.reg_mov(pc, 0)

                    def cond():
                        g.reg_add(pc, pc, 16)
                        g.dma_start(out=flg1[:, :],
                                    in_=flags.ap()[:, bass.ds(pa, 1)]).then_inc(sPOLL, 16)
                        g.wait_ge(sPOLL, pc)
                        vv = g.value_load(flg1[0:1, 0:1], min_val=0)
                        return vv - nonce_v

                    with g.While(cond):
                        g.nop()
                g.sem_inc(sRDY.h, 1)
                sRDY.bump()
            return sRDY.n

        def emit_attention(rdy_tag):
            nonlocal prev
            ps = ExitStack()
            with ps:
                Ep = ps.enter_context
                Qf = Ep(nc.sbuf_tensor("Qf", [128, DC, QB], BF16))
                Ob = Ep(nc.sbuf_tensor("Ob", [128, DC, QB], BF16))
                Obig = Ep(nc.sbuf_tensor("Obig", [128, DC, QB], F32))
                Kf = Ep(nc.sbuf_tensor("Kf", [128, DC, KC], BF16))
                Zt = Ep(nc.sbuf_tensor("Zt", [128, 8, D], BF16))
                Pt = Ep(nc.sbuf_tensor("Pt", [128, 8, QB], BF16))
                QKs = Ep(nc.sbuf_tensor("QKs", [128, DC, D], BF16))
                VOs = Ep(nc.sbuf_tensor("VOs", [128, DC, D], BF16))
                x1b = Ep(nc.sbuf_tensor("x1b", [128, 2, 1024], F32))
                x2e = Ep(nc.sbuf_tensor("x2e", [128, 2, 1024], F32))
                ztm3 = Ep(nc.sbuf_tensor("ztm3", [128, 2, 1024], BF16))
                z3T = Ep(nc.sbuf_tensor("z3Tb", [128, 2, DC, 128], BF16))
                sqb3 = Ep(nc.sbuf_tensor("sqb3", [128, 1024], F32))
                st3s = Ep(nc.sbuf_tensor("st3s", [128, 8, 2], F32))
                lrow = Ep(nc.sbuf_tensor("lrow", [1, QB], F32))
                rlrow = Ep(nc.sbuf_tensor("rlrow", [1, QB], F32))
                psS = [Ep(nc.psum_tensor(f"psS{i_}", [128, 512], F32)) for i_ in range(2)]
                psV = [Ep(nc.psum_tensor(f"psV{i_}", [128, 512], F32)) for i_ in range(2)]
                psL = Ep(nc.psum_tensor("psL", [1, 512], F32))
                psQ = [Ep(nc.psum_tensor(f"psQ{i_}", [128, 512], F32)) for i_ in range(2)]
                pz2 = Ep(nc.psum_tensor("pz2", [128, 1024], BF16))

                @block.gpsimd
                def _(g):
                    barrier(g, prev, skip=("g",))
                    g.dma_start(out=QKs[:, :, :],
                                in_=qk_in.ap().rearrange("(p c) o -> p c o", p=128)).then_inc(sG.h, 16)
                    T.set(("at", "qk"), sG.bump())
                    g.dma_start(out=VOs[:, :, :],
                                in_=vo_in.ap().rearrange("(p c) o -> p c o", p=128)).then_inc(sG.h, 16)
                    T.set(("at", "vo"), sG.bump())

                for qb in range(NQB):
                    a = ("at", qb)

                    @block.scalar
                    def _(s):
                        barrier(s, prev, skip=("ac", "x"))
                        s.dma_start(out=Pt[:, :, :],
                                    in_=z2fm_loc.ap().rearrange("(c p) n -> p c n", p=128)[:, :, qb * QB:(qb + 1) * QB]
                                    ).then_inc(sX.h, 16)
                        T.set((a, "zq"), sX.bump())
                        for oc in range(DC):
                            for qs in range(2):
                                s.wait_ge(sPE.h, T.get((a, "qmm", oc, qs)))
                                s.copy(Qf[:, oc, 512 * qs:512 * (qs + 1)],
                                       psQ[qs][:, :]).then_inc(sAC.h, 1)
                                T.set((a, "qev", oc, qs), sAC.bump())
                        for kc in range(NKC):
                            if kc == 0:
                                s.wait_ge(sPE.h, T.get((a, "qmm", DC - 1, 1)))
                            else:
                                s.wait_ge(sPE.h, T.get((a, "avl", kc - 1)))
                            for uc in range(8):
                                for qs in range(2):
                                    su = 2 * uc + qs
                                    s.wait_ge(sPE.h, T.get((a, "smm", kc, uc, qs)))
                                    s.activation(Pt[:, uc, 512 * qs:512 * (qs + 1)],
                                                 psS[su % 2][:, :], AF.Exp,
                                                 scale=1.0 / 32.0).then_inc(sAC.h, 1)
                                    T.set((a, "exp", kc, uc, qs), sAC.bump())
                        for qt in range(8):
                            if qt >= 2:
                                s.wait_ge(sVE.h, T.get((a, "x2", qt - 2, 1)))
                            s.dma_start(out=x1b[:, qt % 2, :],
                                        in_=x1_loc.ap()[qb * QB + 128 * qt: qb * QB + 128 * (qt + 1), :]
                                        ).then_inc(sX.h, 16)
                            T.set((a, "x1", qt), sX.bump())
                            s.wait_ge(sVE.h, T.get((a, "x2", qt, 1)))
                            if qt >= 1:
                                s.wait_ge(sVE.h, T.get((a, "var3", qt - 1)))
                            s.activation(sqb3[:, :], x2e[:, qt % 2, :], AF.Square).then_inc(sAC.h, 1)
                            T.set((a, "sq3", qt), sAC.bump())
                            s.wait_ge(sVE.h, T.get((a, "var3", qt)))
                            s.activation(st3s[:, qt, 1:2], st3s[:, qt, 0:1], AF.Sqrt,
                                         bias=epsb[:, 0:1]).then_inc(sAC.h, 1)
                            T.set((a, "sqrt3", qt), sAC.bump())
                            s.wait_ge(sPE.h, T.get((a, "ztr3", qt)))
                            if qt >= 2:
                                s.wait_ge(sG.h, T.get((a, "z3st", qt - 2)))
                            s.copy(z3T[:, qt % 2, :, :],
                                   pz2.ap().rearrange("p (c q) -> p c q", q=128)).then_inc(sAC.h, 1)
                            T.set((a, "z3ev", qt), sAC.bump())

                    @block.sync
                    def _(sy):
                        sy.wait_ge(sRDY.h, rdy_tag)
                        for kc in range(NKC):
                            if qb > 0 or kc > 0:
                                pa_ = (a, "sl", kc - 1) if kc > 0 else (("at", qb - 1), "sl", NKC - 1)
                                sy.wait_ge(sPE.h, T.get(pa_))
                            sy.dma_start(out=Kf[:, :, :],
                                         in_=g_fm.ap().rearrange("(c p) n -> p c n", p=128)[:, :, kc * KC:(kc + 1) * KC]
                                         ).then_inc(sW.h, 16)
                            T.set((a, "kf", kc), sW.bump())
                            if qb > 0 or kc > 0:
                                pa_ = (a, "avl", kc - 1) if kc > 0 else (("at", qb - 1), "avl", NKC - 1)
                                sy.wait_ge(sPE.h, T.get(pa_))
                            sy.dma_start(out=Zt[:, :, :],
                                         in_=g_tm.ap().rearrange("(u p) d -> p u d", p=128)[:, 8 * kc:8 * (kc + 1), :]
                                         ).then_inc(sW.h, 16)
                            T.set((a, "zt", kc), sW.bump())

                    @block.tensor
                    def _(t):
                        barrier(t, prev, skip=("pe",))
                        t.wait_ge(sG.h, T.get(("at", "qk")))
                        t.wait_ge(sX.h, T.get((a, "zq")))
                        for oc in range(DC):
                            for qs in range(2):
                                if 2 * oc + qs >= 2:
                                    k = 2 * oc + qs - 2
                                    t.wait_ge(sAC.h, T.get((a, "qev", k // 2, k % 2)))
                                for kcc in range(DC):
                                    ins = t.matmul(psQ[qs][:, :],
                                                   QKs[:, kcc, 128 * oc:128 * (oc + 1)],
                                                   Pt[:, kcc, 512 * qs:512 * (qs + 1)],
                                                   start=(kcc == 0), stop=(kcc == DC - 1))
                                ins.then_inc(sPE.h, 1)
                                T.set((a, "qmm", oc, qs), sPE.bump())
                        t.wait_ge(sAC.h, T.get((a, "qev", DC - 1, 1)))
                        t.wait_ge(sG.h, T.get(("at", "vo")))
                        for kc in range(NKC):
                            t.wait_ge(sW.h, T.get((a, "kf", kc)))
                            for uc in range(8):
                                for qs in range(2):
                                    su = 2 * uc + qs
                                    if kc > 0 or su >= 2:
                                        pa_ = (a, "exp", kc, (su - 2) // 2, (su - 2) % 2) if su >= 2 \
                                            else (a, "exp", kc - 1, 7, 1)
                                        t.wait_ge(sAC.h, T.get(pa_))
                                    for dcc in range(DC):
                                        ins = t.matmul(psS[su % 2][:, :],
                                                       Kf[:, dcc, 128 * uc:128 * (uc + 1)],
                                                       Qf[:, dcc, 512 * qs:512 * (qs + 1)],
                                                       start=(dcc == 0), stop=(dcc == DC - 1))
                                    ins.then_inc(sPE.h, 1)
                                    T.set((a, "smm", kc, uc, qs), sPE.bump())
                            T.set((a, "sl", kc), sPE.n)
                            t.wait_ge(sW.h, T.get((a, "zt", kc)))
                            t.wait_ge(sAC.h, T.get((a, "exp", kc, 7, 1)))
                            for dcc in range(DC):
                                for qs in range(2):
                                    sv = 2 * dcc + qs
                                    if kc > 0 or sv >= 2:
                                        pa_ = (a, "oev", kc, (sv - 2) // 2, (sv - 2) % 2) if sv >= 2 \
                                            else (a, "oev", kc - 1, 7, 1)
                                        t.wait_ge(sVE.h, T.get(pa_))
                                    for uc in range(8):
                                        ins = t.matmul(psV[sv % 2][:, :],
                                                       Zt[:, uc, 128 * dcc:128 * (dcc + 1)],
                                                       Pt[:, uc, 512 * qs:512 * (qs + 1)],
                                                       start=(uc == 0), stop=(uc == 7))
                                    ins.then_inc(sPE.h, 1)
                                    T.set((a, "vmm", kc, dcc, qs), sPE.bump())
                            for qs in range(2):
                                if kc > 0 or qs >= 1:
                                    pa_ = (a, "lev", kc, qs - 1) if qs >= 1 else (a, "lev", kc - 1, 1)
                                    t.wait_ge(sVE.h, T.get(pa_))
                                for uc in range(8):
                                    ins = t.matmul(psL[:, :], onesb[:, :],
                                                   Pt[:, uc, 512 * qs:512 * (qs + 1)],
                                                   start=(uc == 0), stop=(uc == 7))
                                ins.then_inc(sPE.h, 1)
                                T.set((a, "lmm", kc, qs), sPE.bump())
                            T.set((a, "avl", kc), sPE.n)
                        t.wait_ge(sVE.h, T.get((a, "recip")))
                        for qs in range(2):
                            t.matmul(psQ[qs][:, :], ones1[0:1, :], rlrow[0:1, 512 * qs:512 * (qs + 1)],
                                     start=True, stop=True).then_inc(sPE.h, 1)
                            T.set((a, "rlb", qs), sPE.bump())
                        t.wait_ge(sVE.h, T.get((a, "obm", DC - 1, 1)))
                        for qt in range(8):
                            for h2 in range(2):
                                sv = 2 * qt + h2
                                if sv >= 2:
                                    t.wait_ge(sVE.h, T.get((a, "x2", (sv - 2) // 2, (sv - 2) % 2)))
                                for dcc in range(DC):
                                    ins = t.matmul(psQ[sv % 2][:, :],
                                                   Ob[:, dcc, 128 * qt:128 * (qt + 1)],
                                                   VOs[:, dcc, 512 * h2:512 * (h2 + 1)],
                                                   start=(dcc == 0), stop=(dcc == DC - 1))
                                ins.then_inc(sPE.h, 1)
                                T.set((a, "vomm", qt, h2), sPE.bump())
                            t.wait_ge(sVE.h, T.get((a, "z3", qt)))
                            if qt >= 1:
                                t.wait_ge(sAC.h, T.get((a, "z3ev", qt - 1)))
                            for dcc in range(DC):
                                ins = t.transpose(pz2[:, 128 * dcc:128 * (dcc + 1)],
                                                  ztm3[:, qt % 2, 128 * dcc:128 * (dcc + 1)],
                                                  identb[:, :])
                            ins.then_inc(sPE.h, 1)
                            T.set((a, "ztr3", qt), sPE.bump())

                    @block.vector
                    def _(v):
                        def CH(ins):
                            ins.then_inc(sVE.h, 1)
                            c = sVE.bump()
                            v.wait_ge(sVE.h, c)
                            return c
                        barrier(v, prev, skip=("ve",))
                        CH(v.memset(Obig[:, :, :], 0.0))
                        CH(v.memset(lrow[:, :], 0.0))
                        for kc in range(NKC):
                            for dcc in range(DC):
                                for qs in range(2):
                                    v.wait_ge(sPE.h, T.get((a, "vmm", kc, dcc, qs)))
                                    v.tensor_add(Obig[:, dcc, 512 * qs:512 * (qs + 1)],
                                                 Obig[:, dcc, 512 * qs:512 * (qs + 1)],
                                                 psV[(2 * dcc + qs) % 2][:, :]).then_inc(sVE.h, 1)
                                    T.set((a, "oev", kc, dcc, qs), sVE.bump())
                            for qs in range(2):
                                v.wait_ge(sPE.h, T.get((a, "lmm", kc, qs)))
                                v.tensor_add(lrow[0:1, 512 * qs:512 * (qs + 1)],
                                             lrow[0:1, 512 * qs:512 * (qs + 1)],
                                             psL[0:1, :]).then_inc(sVE.h, 1)
                                T.set((a, "lev", kc, qs), sVE.bump())
                        v.wait_ge(sVE.h, sVE.n)
                        v.reciprocal(rlrow[:, :], lrow[:, :]).then_inc(sVE.h, 1)
                        T.set((a, "recip"), sVE.bump())
                        for dcc in range(DC):
                            for qs in range(2):
                                v.wait_ge(sPE.h, T.get((a, "rlb", qs)))
                                v.tensor_tensor(Ob[:, dcc, 512 * qs:512 * (qs + 1)],
                                                Obig[:, dcc, 512 * qs:512 * (qs + 1)],
                                                psQ[qs][:, :], ALU.mult).then_inc(sVE.h, 1)
                                T.set((a, "obm", dcc, qs), sVE.bump())
                        for qt in range(8):
                            if qt >= 2:
                                v.wait_ge(sG.h, T.get((a, "x2st", qt - 2)))
                                v.wait_ge(sAC.h, T.get((a, "sq3", qt - 2)))
                            for h2 in range(2):
                                v.wait_ge(sPE.h, T.get((a, "vomm", qt, h2)))
                                if h2 == 0:
                                    v.wait_ge(sX.h, T.get((a, "x1", qt)))
                                ins = v.tensor_add(x2e[:, qt % 2, 512 * h2:512 * (h2 + 1)],
                                                   x1b[:, qt % 2, 512 * h2:512 * (h2 + 1)],
                                                   psQ[(2 * qt + h2) % 2][:, :])
                                T.set((a, "x2", qt, h2), CH(ins))
                            sidx = qb * 8 + qt
                            CH(v.tensor_reduce(st3s[:, qt, 0:1], x2e[:, qt % 2, :], AX.X, ALU.add))
                            CH(v.tensor_scalar(stats3[:, sidx, 0:1], st3s[:, qt, 0:1], -1.0 / D, None, ALU.mult))
                            v.wait_ge(sAC.h, T.get((a, "sq3", qt)))
                            CH(v.tensor_reduce(st3s[:, qt, 0:1], sqb3[:, :], AX.X, ALU.add))
                            CH(v.tensor_scalar(st3s[:, qt, 0:1], st3s[:, qt, 0:1], 1.0 / D, None, ALU.mult))
                            CH(v.tensor_tensor(st3s[:, qt, 1:2], stats3[:, sidx, 0:1], stats3[:, sidx, 0:1], ALU.mult))
                            T.set((a, "var3", qt),
                                  CH(v.tensor_sub(st3s[:, qt, 0:1], st3s[:, qt, 0:1], st3s[:, qt, 1:2])))
                            v.wait_ge(sAC.h, T.get((a, "sqrt3", qt)))
                            CH(v.reciprocal(stats3[:, sidx, 1:2], st3s[:, qt, 1:2]))
                            if qt >= 2:
                                v.wait_ge(sPE.h, T.get((a, "ztr3", qt - 2)))
                            ins = v.tensor_scalar(ztm3[:, qt % 2, :], x2e[:, qt % 2, :],
                                                  stats3[:, sidx, 0:1], stats3[:, sidx, 1:2],
                                                  ALU.add, ALU.mult)
                            ins.then_inc(sVE.h, 1)
                            T.set((a, "z3", qt), sVE.bump())

                    @block.gpsimd
                    def _(g):
                        for qt in range(8):
                            g.wait_ge(sAC.h, T.get((a, "z3ev", qt)))
                            g.dma_start(
                                out=z3_fm.ap().rearrange("(c p) n -> p c n", p=128)[:, :, qb * QB + 128 * qt: qb * QB + 128 * (qt + 1)],
                                in_=z3T[:, qt % 2, :, :]).then_inc(sG.h, 16)
                            T.set((a, "z3st", qt), sG.bump())
                            g.wait_ge(sVE.h, T.get((a, "x2", qt, 1)))
                            g.dma_start(out=x2_loc.ap()[qb * QB + 128 * qt: qb * QB + 128 * (qt + 1), :],
                                        in_=x2e[:, qt % 2, :]).then_inc(sG.h, 16)
                            T.set((a, "x2st", qt), sG.bump())

                    prev = snap()

        if mode in ("ffn1", "ffn1dbg"):
            emit_ffn_block("f1b0", {"kind": "ffn1", "blk": 0, "qblock": True, "xrow0": 0, "bpos": 0})
            emit_ffn_block("f1b1", {"kind": "ffn1", "blk": 1, "qblock": True, "xrow0": TBLK, "bpos": 1})

            @block.gpsimd
            def _(g):
                g.wait_ge(sW.h, sW.n)
                for t_ in range(32):
                    g.dma_start(out=out_d.ap()[128 * t_:128 * (t_ + 1), :],
                                in_=x1_loc.ap()[128 * t_:128 * (t_ + 1), :]).then_inc(sG.h, 16)
                    sG.bump()
                g.wait_ge(sG.h, sG.n)
        else:
            emit_ffn_block("f1b0", {"kind": "ffn1", "blk": 0, "qblock": True, "xrow0": 0, "bpos": 0})
            emit_ffn_block("f1b1", {"kind": "ffn1", "blk": 1, "qblock": True, "xrow0": TBLK, "bpos": 1})
            emit_ffn_block("f1b2", {"kind": "ffn1", "blk": 0, "qblock": False, "xrow0": 4096, "bpos": 2})
            emit_ffn_block("f1b3", {"kind": "ffn1", "blk": 1, "qblock": False, "xrow0": 4096 + TBLK, "bpos": 3})
            rdy_tag = emit_pair_barrier()
            emit_attention(rdy_tag)
            if mode == "attn":
                @block.gpsimd
                def _(g):
                    g.wait_ge(sG.h, sG.n)
                    for t_ in range(32):
                        g.dma_start(out=out_d.ap()[128 * t_:128 * (t_ + 1), :],
                                    in_=x2_loc.ap()[128 * t_:128 * (t_ + 1), :]).then_inc(sG.h, 16)
                        sG.bump()
                    g.wait_ge(sG.h, sG.n)
            else:
                emit_ffn_block("f2b0", {"kind": "ffn2", "blk": 0, "xrow0": 0})
                emit_ffn_block("f2b1", {"kind": "ffn2", "blk": 1, "xrow0": TBLK})

                @block.gpsimd
                def _(g):
                    g.wait_ge(sW.h, sW.n)
                    g.wait_ge(sG.h, sG.n)

    return nc


def _prep_inputs(inputs, mode="full"):
    def perm_cp(a, p=128):
        r = a.shape[-2]
        c = r // p
        return np.ascontiguousarray(
            a.reshape(*a.shape[:-2], c, p, a.shape[-1]).swapaxes(-3, -2)
            .reshape(*a.shape))

    x = np.ascontiguousarray(np.asarray(inputs["L0_attn_x"], dtype=np.float32))
    gw = perm_cp(np.ascontiguousarray(np.asarray(inputs["L0_ffn_gate_w"], np.float32).T))
    gb = np.ascontiguousarray(np.asarray(inputs["L0_ffn_gate_bias"], np.float32).reshape(1, 16))
    Wg = np.asarray(inputs["L0_ffn_Wg"], np.float32)
    Wu = np.asarray(inputs["L0_ffn_Wu"], np.float32)
    Wd = np.asarray(inputs["L0_ffn_Wd"], np.float32)
    Wsg = np.asarray(inputs["L0_ffn_Ws_g"], np.float32).T
    Wsu = np.asarray(inputs["L0_ffn_Ws_u"], np.float32).T
    Wsd = np.asarray(inputs["L0_ffn_Ws_d"], np.float32).T
    wg_all = perm_cp(np.concatenate(
        [Wg.transpose(0, 2, 1), Wsg[None, :, 0:I], Wsg[None, :, I:2 * I]], axis=0))
    wu_all = perm_cp(np.concatenate(
        [Wu.transpose(0, 2, 1), Wsu[None, :, 0:I], Wsu[None, :, I:2 * I]], axis=0))
    wd_all = perm_cp(np.concatenate(
        [Wd.transpose(0, 2, 1), Wsd[None, 0:I, :], Wsd[None, I:2 * I, :]], axis=0))
    qk = perm_cp(np.ascontiguousarray(np.asarray(inputs["L0_attn_QK"], np.float32)))
    vo = perm_cp(np.ascontiguousarray(np.asarray(inputs["L0_attn_VO"], np.float32)))
    ident = np.eye(128, dtype=np.float32)
    nonce = int(np.random.default_rng().integers(1, 2**31 - 1))

    xt = x.reshape(2, 4, 4096, D)
    in_maps = []
    for c in range(8):
        b, r = c // 4, c % 4
        p = c % 2
        kh = (r + 2) % 4
        xin = np.ascontiguousarray(np.concatenate([xt[b, r], xt[b, kh]], axis=0))
        role = np.zeros((1, 16), np.uint32)
        role[0, 0] = nonce
        role[0, 1] = p
        role[0, 2] = 1 - p
        in_maps.append({
            "x_in": xin, "gw": gw, "gb": gb,
            "wg_all": wg_all, "wu_all": wu_all, "wd_all": wd_all,
            "qk": qk, "vo": vo, "ident": ident, "role": role,
        })
    return in_maps


_NC_CACHE = {}


def _get_nc(mode="full"):
    if mode not in _NC_CACHE:
        T = Tags()
        build_program(T, mode)
        T.final = True
        nc = build_program(T, mode)
        lower_extended_insts(nc)
        _NC_CACHE[mode] = nc
    return _NC_CACHE[mode]


def kernel(**inputs):
    nc = _get_nc("full")
    in_maps = _prep_inputs(inputs)
    res = run_bass_kernel_spmd(nc, in_maps, core_ids=list(range(8)))
    out = np.empty((2, 16384, D), np.float32)
    for c in range(8):
        b, r = c // 4, c % 4
        out[b, 4096 * r:4096 * (r + 1), :] = res.results[c]["out"]
    return out


if __name__ == "__main__":
    for m in ("ffn1",):
        T = Tags()
        nc1 = build_program(T, m)
        T.final = True
        nc2 = build_program(T, m)
        print(m, "built OK")
